# version 18
# baseline (speedup 1.0000x reference)
import contextlib
import numpy as np
import ml_dtypes

import concourse.bass as bass
import concourse.mybir as mybir
from concourse.bass_utils import run_bass_kernel_spmd

F32 = mybir.dt.float32
BF16 = mybir.dt.bfloat16
AF = mybir.ActivationFunctionType
ALU = mybir.AluOpType
AX = mybir.AxisListType

D = 4096
NDT = 32
TCORE = 1024
TEXT = 1536
NTILE = 12
NH = 32
NKV = 8
HD = 128
SCALE = 1.0 / float(np.sqrt(128.0))
NK = 256


def build_program(dump=(), phases=("A","A2","B","C","D","E"), a2s=5):
    nc = bass.Bass("TRN2", target_bir_lowering=False)

    x_ext = nc.dram_tensor("x_ext", [TEXT, D], F32, kind="ExternalInput").ap()
    xT_core = nc.dram_tensor("xT_core", [D, TCORE], F32, kind="ExternalInput").ap()
    wq_pre = nc.dram_tensor("wq_pre", [32, 128, D], F32, kind="ExternalInput").ap()
    wk_pre = nc.dram_tensor("wk_pre", [8, 128, D], F32, kind="ExternalInput").ap()
    wv_pre = nc.dram_tensor("wv_pre", [8, 128, D], F32, kind="ExternalInput").ap()
    wo_pre = nc.dram_tensor("wo_pre", [32, 128, D], F32, kind="ExternalInput").ap()
    aT_pre = nc.dram_tensor("aT_pre", [128, D], F32, kind="ExternalInput").ap()
    bflat = nc.dram_tensor("bflat", [128, D], F32, kind="ExternalInput").ap()
    routeT_pre = nc.dram_tensor("routeT_pre", [128, 256], F32, kind="ExternalInput").ap()
    emat = nc.dram_tensor("emat", [8, 128], F32, kind="ExternalInput").ap()
    id_bf = nc.dram_tensor("id_bf", [128, 128], BF16, kind="ExternalInput").ap()
    id_f32 = nc.dram_tensor("id_f32", [128, 128], F32, kind="ExternalInput").ap()
    id8 = nc.dram_tensor("id8", [8, 8], F32, kind="ExternalInput").ap()
    amask_pre = nc.dram_tensor("amask_pre", [128, 5, 640], BF16, kind="ExternalInput").ap()
    out = nc.dram_tensor("out", [D, TCORE], F32, kind="ExternalOutput").ap()

    dbg = {}
    def dbg_out(name, shape, dtype=F32):
        if name in dump:
            dbg[name] = nc.dram_tensor("dbg_" + name, shape, dtype, kind="ExternalOutput").ap()
            return dbg[name]
        return None

    d_xnT = dbg_out("xnT", [128, 32 * 1024], BF16)
    d_xnE = dbg_out("xnE", [128, 16 * 1024], BF16)
    d_whT = dbg_out("whT", [128, 1024], BF16)
    d_KT = dbg_out("KT", [128, 8, TEXT], BF16)
    d_V = dbg_out("V", [128, 12, 1024], BF16)
    d_QT = dbg_out("QT", [128, 32 * 1024], BF16)
    d_oT = dbg_out("oT", [128, 32 * 1024], BF16)

    with contextlib.ExitStack() as stP:
        def sb(name, shape, dtype, side=None):
            return stP.enter_context(nc.sbuf_tensor(name, shape, dtype, side=side))

        def sems(*names):
            return [stP.enter_context(nc.semaphore(n)) for n in names]

        arenaA = sb("arenaA", [128, 32 * 1024], BF16, side="right")
        arenaB = sb("arenaB", [128, 32 * 1024], BF16)

        def xnc(d0):
            return arenaA[:, d0 * 1024:(d0 + 1) * 1024]
        def xne(d0):
            return arenaB[:, d0 * 512:(d0 + 1) * 512]
        def qt_(h):
            return arenaB[:, h * 1024:(h + 1) * 1024]

        whT = sb("whT", [128, 1024], BF16)
        s_idbf = sb("s_idbf", [128, 128], BF16)
        s_idf = sb("s_idf", [128, 128], F32)
        s_id8 = sb("s_id8", [8, 8], F32)
        s_emat = sb("s_emat", [8, 128], F32)
        s_eps = sb("s_eps", [128, 1], F32)

        (smCL, smINIT, smVA, smVB, smSA, smVC, smVD, smSB, smTT, smVE) = sems(
            "smCL", "smINIT", "smVA", "smVB", "smSA", "smVC", "smVD", "smSB", "smTT", "smVE")
        smXL = sems("smXL0", "smXL1")
        smSE, = sems("smSE")
        (smGL, smT2, smV2) = sems("smGL", "smT2", "smV2")
        (smKT_, smKE, smKF, smVS, smTV, smVEv) = sems(
            "smKT", "smKE", "smKF", "smVS", "smTV", "smVEv")
        smKL = sems("smKL0", "smKL1")
        (smQT_, smQE, smQF) = sems("smQT", "smQE", "smQF")
        smQL = sems("smQL0", "smQL1")
        (smDM, smS, smEXP, smSTT, smRC, smPN, smTR, smPTE, smO, smOE) = sems(
            "smDM", "smS", "smEXP", "smSTT", "smRC", "smPN", "smTR", "smPTE", "smO", "smOE")
        (smWF, smET, smEA, smBF) = sems("smWF", "smET", "smEA", "smBF")
        smWL = sems("smWL0", "smWL1")
        smXT = sems("smXT0", "smXT1")
        smOUT = sems("smOUT0", "smOUT1")
        smDBG, = sems("smDBG")
        _dbgc = [0]

        with contextlib.ExitStack() as stA:
            xb = [stA.enter_context(nc.sbuf_tensor(f"xb{i}", [128, D], F32)) for i in range(2)]
            xn_tok = [stA.enter_context(nc.sbuf_tensor(f"xn_tok{i}", [128, D], BF16))
                      for i in range(2)]
            bnst = [stA.enter_context(nc.sbuf_tensor(f"bnst{i}", [128, 48], F32)) for i in range(2)]
            bnag = [stA.enter_context(nc.sbuf_tensor(f"bnag{i}", [128, 2], F32)) for i in range(2)]
            stds = [stA.enter_context(nc.sbuf_tensor(f"stds{i}", [128, 1], F32)) for i in range(2)]
            rstd = [stA.enter_context(nc.sbuf_tensor(f"rstd{i}", [128, 1], F32)) for i in range(2)]
            negmr = [stA.enter_context(nc.sbuf_tensor(f"negmr{i}", [128, 1], F32)) for i in range(2)]
            pbank = [stA.enter_context(nc.psum_tensor(f"pbank{i}", [128, 1024], BF16)) for i in range(2)]

            def xnT_dest(u, g):
                if u < 4:
                    buf, width, tcol = arenaB, 512, u * 128
                else:
                    buf, width, tcol = arenaA, 1024, (u - 4) * 128
                full = buf[:, 0:32 * width].rearrange("p (a w) -> p a w", w=width)
                return full[:, 8 * g:8 * g + 8, tcol:tcol + 128]

            with nc.Block("phA", no_gpsimd_drain=True) as blk:
                if "A" in phases:

                    @blk.sync
                    def _(sync):
                        sync.dma_start(out=s_idbf[:], in_=id_bf[:]).then_inc(smCL, 16)
                        sync.dma_start(out=s_idf[:], in_=id_f32[:]).then_inc(smCL, 16)
                        sync.dma_start(out=s_id8[:], in_=id8[:]).then_inc(smCL, 16)
                        sync.dma_start(out=s_emat[:], in_=emat[:]).then_inc(smCL, 16)
                        for t in range(NTILE):
                            if t >= 2:
                                sync.wait_ge(smVB, t - 1)
                                sync.wait_ge(smSB, t - 1)
                            sync.dma_start(out=xb[t % 2][:],
                                           in_=x_ext[t * 128:(t + 1) * 128, :]
                                           ).then_inc(smXL[t % 2], 16)

                    @blk.vector
                    def _(v):
                        v.memset(s_eps[:], 1e-5).then_inc(smINIT, 1)

                        def ev_even(u):
                            for g in range(0, 4, 2):
                                G = 4 * u + g
                                v.wait_ge(smTT, G + 1)
                                v.tensor_copy(
                                    xnT_dest(u, g),
                                    pbank[G % 2][:].rearrange("p (a w) -> p a w", w=128)
                                ).then_inc(smVE, 1)

                        for t in range(NTILE):
                            v.wait_ge(smXL[t % 2], 16 * (t // 2 + 1))
                            for c in range(8):
                                ins = v.bn_stats(bnst[t % 2][:, c * 6:(c + 1) * 6],
                                                 xb[t % 2][:, c * 512:(c + 1) * 512])
                                if c == 7:
                                    ins.then_inc(smVA, 1)
                            v.wait_ge(smVA, t + 1)
                            if t >= 2:
                                v.wait_ge(smSA, t - 1)
                            v.bn_aggr(bnag[t % 2][:], bnst[t % 2][:]).then_inc(smVB, 1)
                            if t >= 1:
                                u = t - 1
                                v.wait_ge(smSA, t)
                                if t >= 3:
                                    v.wait_ge(smSB, t - 2)
                                v.reciprocal(rstd[u % 2][:], stds[u % 2][:]).then_inc(smVC, 1)
                                v.wait_ge(smVC, t)
                                v.scalar_tensor_tensor(out=negmr[u % 2][:],
                                                       in0=bnag[u % 2][:, 0:1],
                                                       scalar=-1.0, in1=rstd[u % 2][:],
                                                       op0=ALU.mult, op1=ALU.mult
                                                       ).then_inc(smVD, 1)
                            if t >= 2:
                                ev_even(t - 2)
                        u = NTILE - 1
                        v.wait_ge(smSA, NTILE)
                        v.reciprocal(rstd[u % 2][:], stds[u % 2][:]).then_inc(smVC, 1)
                        v.wait_ge(smVC, NTILE)
                        v.scalar_tensor_tensor(out=negmr[u % 2][:], in0=bnag[u % 2][:, 0:1],
                                               scalar=-1.0, in1=rstd[u % 2][:],
                                               op0=ALU.mult, op1=ALU.mult).then_inc(smVD, 1)
                        ev_even(NTILE - 2)
                        ev_even(NTILE - 1)

                    @blk.scalar
                    def _(s):
                        s.wait_ge(smINIT, 1)

                        def ev_odd(u):
                            for g in range(1, 4, 2):
                                G = 4 * u + g
                                s.wait_ge(smTT, G + 1)
                                s.copy(
                                    xnT_dest(u, g),
                                    pbank[G % 2][:].rearrange("p (a w) -> p a w", w=128)
                                ).then_inc(smSE, 1)

                        for t in range(NTILE):
                            s.wait_ge(smVB, t + 1)
                            if t >= 2:
                                s.wait_ge(smVC, t - 1)
                            s.activation(stds[t % 2][:], bnag[t % 2][:, 1:2], AF.Sqrt,
                                         bias=s_eps[:]).then_inc(smSA, 1)
                            if t >= 1:
                                u = t - 1
                                s.wait_ge(smVD, t)
                                if t >= 3:
                                    s.wait_ge(smTT, 4 * (t - 2))
                                s.activation(xn_tok[u % 2][:], xb[u % 2][:], AF.Identity,
                                             bias=negmr[u % 2][:],
                                             scale=rstd[u % 2][:]).then_inc(smSB, 1)
                            if t >= 2:
                                ev_odd(t - 2)
                        ev_odd(NTILE - 2)
                        u = NTILE - 1
                        s.wait_ge(smVD, NTILE)
                        s.wait_ge(smTT, 4 * (NTILE - 1))
                        s.activation(xn_tok[u % 2][:], xb[u % 2][:], AF.Identity,
                                     bias=negmr[u % 2][:],
                                     scale=rstd[u % 2][:]).then_inc(smSB, 1)
                        ev_odd(NTILE - 1)

                    @blk.tensor
                    def _(t_):
                        t_.wait_ge(smCL, 64)
                        for t in range(NTILE):
                            t_.wait_ge(smSB, t + 1)
                            for g in range(4):
                                G = 4 * t + g
                                if G >= 2:
                                    Gp = G - 2
                                    if Gp % 2 == 0:
                                        t_.wait_ge(smVE, Gp // 2 + 1)
                                    else:
                                        t_.wait_ge(smSE, Gp // 2 + 1)
                                for j in range(8):
                                    ins = t_.transpose(
                                        pbank[G % 2][:, j * 128:(j + 1) * 128],
                                        xn_tok[t % 2][:, (8 * g + j) * 128:(8 * g + j + 1) * 128],
                                        s_idbf[:])
                                    if j == 7:
                                        ins.then_inc(smTT, 1)

            if d_xnT is not None:
                with nc.Block("dbgA", no_gpsimd_drain=True) as blk:
                    @blk.sync
                    def _(sync):
                        sync.dma_start(out=d_xnT, in_=arenaA[:]).then_inc(smDBG, 16)
                        sync.dma_start(out=d_xnE, in_=arenaB[:, 0:16384]).then_inc(smDBG, 16)
                        _dbgc[0] += 32
                        sync.wait_ge(smDBG, _dbgc[0])

        with contextlib.ExitStack() as stA2:
            aT_sb = stA2.enter_context(nc.sbuf_tensor("aT_sb", [128, 32, 128], BF16))
            routeT_sb = stA2.enter_context(nc.sbuf_tensor("routeT_sb", [128, 32, 8], BF16))
            lgT_sb = stA2.enter_context(nc.sbuf_tensor("lgT_sb", [8, 1024], F32))
            lg_tok = stA2.enter_context(nc.sbuf_tensor("lg_tok", [128, 64], F32))
            maxv = stA2.enter_context(nc.sbuf_tensor("maxv", [128, 8], F32))
            eq = stA2.enter_context(nc.sbuf_tensor("eq", [128, 64], F32))
            cnt = stA2.enter_context(nc.sbuf_tensor("cnt", [128, 8], F32))
            rc_ = stA2.enter_context(nc.sbuf_tensor("rc_", [128, 8], F32))
            rcm = stA2.enter_context(nc.sbuf_tensor("rcm", [128, 8], F32))
            rw_tok = stA2.enter_context(nc.sbuf_tensor("rw_tok", [128, 64], F32))
            rwT_sb = stA2.enter_context(nc.sbuf_tensor("rwT_sb", [8, 1024], F32))
            rw_er = stA2.enter_context(nc.sbuf_tensor("rw_er", [128, 1024], F32))
            pRW8 = stA2.enter_context(nc.psum_tensor("pRW8", [8, 1024], F32))
            pSm = stA2.enter_context(nc.psum_tensor("pSm", [128, 512], F32))
            pBig1 = stA2.enter_context(nc.psum_tensor("pBig1", [128, 1024], F32))
            pBig2 = stA2.enter_context(nc.psum_tensor("pBig2", [128, 1024], F32))

            with nc.Block("phA2", no_gpsimd_drain=True) as blk:
                if "A2" in phases:

                    @blk.gpsimd
                    def _(g):
                        g.dma_start(out=aT_sb[:],
                                    in_=aT_pre.rearrange("p (j n) -> p j n", n=128)).then_inc(smGL, 16)
                        g.dma_start(out=routeT_sb[:],
                                    in_=routeT_pre.rearrange("p (j n) -> p j n", n=8)).then_inc(smGL, 16)

                    @blk.tensor
                    def _(t_):
                        t_.wait_ge(smCL, 64)
                        t_.wait_ge(smGL, 32)
                        for d0 in range(NDT):
                            for half in range(2):
                                ins = t_.matmul(pRW8[:, half * 512:(half + 1) * 512],
                                                routeT_sb[:, d0, :],
                                                xnc(d0)[:, half * 512:(half + 1) * 512],
                                                start=(d0 == 0), stop=(d0 == NDT - 1))
                                if d0 == NDT - 1 and half == 1:
                                    ins.then_inc(smT2, 1)
                        if a2s >= 2:
                            t_.wait_ge(smV2, 1)
                            for tt in range(8):
                                ins = t_.transpose(pSm[:, tt * 8:(tt + 1) * 8],
                                                   lgT_sb[:, tt * 128:(tt + 1) * 128], s_id8[:])
                                if tt == 7:
                                    ins.then_inc(smT2, 1)
                        if a2s >= 3:
                            t_.wait_ge(smV2, 8)
                            for tt in range(8):
                                ins = t_.transpose(pRW8[0:8, tt * 128:(tt + 1) * 128],
                                                   rw_tok[:, tt * 8:(tt + 1) * 8], s_idf[:])
                                if tt == 7:
                                    ins.then_inc(smT2, 1)
                            t_.wait_ge(smV2, 9)
                            for half in range(2):
                                ins = t_.matmul(pBig1[:, half * 512:(half + 1) * 512], s_emat[:],
                                                rwT_sb[:, half * 512:(half + 1) * 512],
                                                start=True, stop=True)
                                if half == 1:
                                    ins.then_inc(smT2, 1)
                        if a2s >= 4:
                            for d0 in range(NDT):
                                for half in range(2):
                                    ins = t_.matmul(pBig2[:, half * 512:(half + 1) * 512],
                                                    aT_sb[:, d0, :],
                                                    xnc(d0)[:, half * 512:(half + 1) * 512],
                                                    start=(d0 == 0), stop=(d0 == NDT - 1))
                                    if d0 == NDT - 1 and half == 1:
                                        ins.then_inc(smT2, 1)

                    @blk.vector
                    def _(v):
                        v.wait_ge(smT2, 1)
                        v.tensor_copy(lgT_sb[:], pRW8[:]).then_inc(smV2, 1)
                        if a2s < 2:
                            return
                        v.wait_ge(smT2, 2)
                        v.tensor_copy(lg_tok[:], pSm[:, 0:64]).then_inc(smV2, 1)
                        v.wait_ge(smV2, 2)
                        lg3 = lg_tok[:].rearrange("p (g e) -> p g e", e=8)
                        v.tensor_reduce(out=maxv[:].rearrange("p (g e) -> p g e", e=1),
                                        in_=lg3, axis=AX.X, op=ALU.max).then_inc(smV2, 1)
                        v.wait_ge(smV2, 3)
                        mx_b = maxv[:].rearrange("p (g e) -> p g e", e=1).broadcast_to((128, 8, 8))
                        v.tensor_tensor(out=eq[:].rearrange("p (g e) -> p g e", e=8),
                                        in0=lg3, in1=mx_b, op=ALU.is_equal).then_inc(smV2, 1)
                        v.wait_ge(smV2, 4)
                        v.tensor_reduce(out=cnt[:].rearrange("p (g e) -> p g e", e=1),
                                        in_=eq[:].rearrange("p (g e) -> p g e", e=8),
                                        axis=AX.X, op=ALU.add).then_inc(smV2, 1)
                        v.wait_ge(smV2, 5)
                        v.reciprocal(rc_[:], cnt[:]).then_inc(smV2, 1)
                        v.wait_ge(smV2, 6)
                        v.tensor_tensor(out=rcm[:], in0=rc_[:], in1=maxv[:],
                                        op=ALU.mult).then_inc(smV2, 1)
                        v.wait_ge(smV2, 7)
                        rc_b = rcm[:].rearrange("p (g e) -> p g e", e=1).broadcast_to((128, 8, 8))
                        v.tensor_tensor(out=rw_tok[:].rearrange("p (g e) -> p g e", e=8),
                                        in0=eq[:].rearrange("p (g e) -> p g e", e=8),
                                        in1=rc_b, op=ALU.mult).then_inc(smV2, 1)
                        if a2s < 3:
                            return
                        v.wait_ge(smT2, 3)
                        v.tensor_copy(rwT_sb[:], pRW8[0:8, :]).then_inc(smV2, 1)
                        v.wait_ge(smT2, 4)
                        v.tensor_copy(rw_er[:], pBig1[:]).then_inc(smV2, 1)
                        if a2s < 4:
                            return
                        v.wait_ge(smT2, 5)
                        v.wait_ge(smV2, 10)
                        v.tensor_tensor(out=whT[:], in0=pBig2[:], in1=rw_er[:],
                                        op=ALU.mult).then_inc(smV2, 1)

            if d_whT is not None:
                with nc.Block("dbgA2", no_gpsimd_drain=True) as blk:
                    @blk.sync
                    def _(sync):
                        sync.dma_start(out=d_whT, in_=whT[:]).then_inc(smDBG, 16)
                        _dbgc[0] += 16
                        sync.wait_ge(smDBG, _dbgc[0])

        with contextlib.ExitStack() as stKV:
            KT = stKV.enter_context(
                nc.sbuf_tensor("KT", [128, 8, TEXT], BF16, side="right"))
            V = stKV.enter_context(
                nc.sbuf_tensor("V", [128, 12, 1024], BF16, side="right"))
            with contextlib.ExitStack() as stB:
                wk_sb = [stB.enter_context(nc.sbuf_tensor(f"wk_sb{i}", [128, 16, 128], BF16))
                         for i in range(2)]
                VT_stage = stB.enter_context(nc.sbuf_tensor("VT_stage", [128, TEXT], BF16))
                pK = [stB.enter_context(nc.psum_tensor(f"pK{i}", [128, 1536], F32)) for i in range(2)]
                pTV = stB.enter_context(nc.psum_tensor("pTV", [128, 1536], BF16))

                def xn_rhs(d0, part):
                    if part == 0:
                        return xne(d0)
                    if part == 1:
                        return xnc(d0)[:, 0:512]
                    return xnc(d0)[:, 512:1024]

                with nc.Block("phB", no_gpsimd_drain=True) as blk:
                    if "B" in phases:

                        @blk.gpsimd
                        def _(g):
                            for H in range(32):
                                p, hf = H // 2, H % 2
                                w_pre = wk_pre[p] if p < 8 else wv_pre[p - 8]
                                if H >= 2:
                                    if hf == 0:
                                        g.wait_ge(smKF, p)
                                    else:
                                        g.wait_ge(smKT_, p)
                                g.dma_start(
                                    out=wk_sb[H % 2][:],
                                    in_=w_pre.rearrange("p (j n) -> p j n", n=128)
                                    [:, hf * 16:(hf + 1) * 16, :]
                                ).then_inc(smKL[H % 2], 16)

                        @blk.tensor
                        def _(t_):
                            for p in range(16):
                                oc = p if p < 8 else p - 8
                                if p >= 2:
                                    t_.wait_ge(smVS, p - 1)
                                for d0 in range(NDT):
                                    H = 2 * p + d0 // 16
                                    if d0 % 16 == 0:
                                        t_.wait_ge(smKL[H % 2], 16 * (H // 2 + 1))
                                    for part in range(3):
                                        ins = t_.matmul(pK[p % 2][:, part * 512:(part + 1) * 512],
                                                        wk_sb[H % 2][:, d0 % 16, :],
                                                        xn_rhs(d0, part),
                                                        start=(d0 == 0), stop=(d0 == NDT - 1))
                                        if d0 == 15 and part == 2:
                                            ins.then_inc(smKF, 1)
                                        if d0 == NDT - 1 and part == 2:
                                            ins.then_inc(smKT_, 1)
                                if p >= 8:
                                    oc_v = p - 8 - 1
                                    if oc_v >= 0:
                                        t_.wait_ge(smVS, p)
                                        if oc_v >= 1:
                                            t_.wait_ge(smVEv, oc_v)
                                        for jt in range(12):
                                            ins = t_.transpose(
                                                pTV[:, jt * 128:(jt + 1) * 128],
                                                VT_stage[:, jt * 128:(jt + 1) * 128],
                                                s_idbf[:])
                                            if jt == 11:
                                                ins.then_inc(smTV, 1)
                            t_.wait_ge(smVS, 16)
                            t_.wait_ge(smVEv, 7)
                            for jt in range(12):
                                ins = t_.transpose(pTV[:, jt * 128:(jt + 1) * 128],
                                                   VT_stage[:, jt * 128:(jt + 1) * 128],
                                                   s_idbf[:])
                                if jt == 11:
                                    ins.then_inc(smTV, 1)

                        @blk.vector
                        def _(v):
                            for p in range(16):
                                v.wait_ge(smKT_, p + 1)
                                if p < 8:
                                    v.tensor_copy(KT[:, p, :], pK[p % 2][:]).then_inc(smVS, 1)
                                else:
                                    oc_v = p - 8
                                    if oc_v >= 1:
                                        v.wait_ge(smTV, oc_v)
                                    v.tensor_copy(VT_stage[:], pK[p % 2][:]).then_inc(smVS, 1)
                                    if oc_v >= 1:
                                        v.tensor_copy(
                                            V[:].rearrange("p a w -> p (a w)")
                                            .rearrange("p (a w) -> p a w", w=1024)
                                            [:, :, (oc_v - 1) * 128:oc_v * 128],
                                            pTV[:].rearrange("p (a w) -> p a w", w=128)
                                        ).then_inc(smVEv, 1)
                            v.wait_ge(smTV, 8)
                            v.tensor_copy(
                                V[:].rearrange("p a w -> p (a w)")
                                .rearrange("p (a w) -> p a w", w=1024)
                                [:, :, 7 * 128:8 * 128],
                                pTV[:].rearrange("p (a w) -> p a w", w=128)
                            ).then_inc(smVEv, 1)

            with contextlib.ExitStack() as stCD:
                wq_sb = [stCD.enter_context(nc.sbuf_tensor(f"wq_sb{i}", [128, 16, 128], BF16))
                         for i in range(2)]
                amask_sb = stCD.enter_context(nc.sbuf_tensor("amask_sb", [128, 5, 640], BF16))
                p_raw = [stCD.enter_context(nc.sbuf_tensor(f"p_raw{i}", [128, 640], BF16))
                         for i in range(2)]
                p_msk = [stCD.enter_context(nc.sbuf_tensor(f"p_msk{i}", [128, 640], BF16))
                         for i in range(2)]
                p_n = [stCD.enter_context(nc.sbuf_tensor(f"p_n{i}", [128, 640], BF16))
                       for i in range(3)]
                pT_sb = [stCD.enter_context(nc.sbuf_tensor(f"pT_sb{i}", [128, 640], BF16))
                         for i in range(3)]
                rowsum = [stCD.enter_context(nc.sbuf_tensor(f"rowsum{i}", [128, 1], F32))
                          for i in range(2)]
                rcp = [stCD.enter_context(nc.sbuf_tensor(f"rcp{i}", [128, 1], F32))
                       for i in range(2)]
                pQ = [stCD.enter_context(nc.psum_tensor(f"pQ{i}", [128, 1024], F32))
                      for i in range(2)]
                pS = stCD.enter_context(nc.psum_tensor("pS", [128, 1024], F32))
                pT = stCD.enter_context(nc.psum_tensor("pT", [128, 1024], BF16))
                pO = stCD.enter_context(nc.psum_tensor("pO", [128, 512], F32))

                def emit_c_step(t_, oc, d0):
                    H = 2 * oc + d0 // 16
                    if d0 % 16 == 0:
                        t_.wait_ge(smQL[H % 2], 16 * (H // 2 + 1))
                    for half in range(2):
                        ins = t_.matmul(pQ[oc % 2][:, half * 512:(half + 1) * 512],
                                        wq_sb[H % 2][:, d0 % 16, :],
                                        xnc(d0)[:, half * 512:(half + 1) * 512],
                                        start=(d0 == 0), stop=(d0 == NDT - 1))
                        if d0 == 15 and half == 1:
                            ins.then_inc(smQF, 1)
                        if d0 == NDT - 1 and half == 1:
                            ins.then_inc(smQT_, 1)

                def emit_s(t_, k):
                    h, q = k // 8, k % 8
                    g = h // 4
                    t_.wait_ge(smQE, h + 1)
                    if k >= 1:
                        t_.wait_ge(smEXP, k)
                    lhs = qt_(h)[:, q * 128:(q + 1) * 128]
                    t_.matmul(pS[:, 0:512], lhs,
                              KT[:, g, q * 128:q * 128 + 512], start=True, stop=True)
                    t_.matmul(pS[:, 512:640], lhs,
                              KT[:, g, q * 128 + 512:q * 128 + 640],
                              start=True, stop=True).then_inc(smS, 1)

                def emit_tr(t_, k):
                    if k >= 1:
                        t_.wait_ge(smPTE, k)
                    t_.wait_ge(smPN, k + 1)
                    for kt in range(5):
                        ins = t_.transpose(pT[:, kt * 128:(kt + 1) * 128],
                                           p_n[k % 3][:, kt * 128:(kt + 1) * 128], s_idbf[:])
                        if kt == 4:
                            ins.then_inc(smTR, 1)

                def emit_o(t_, k):
                    h, q = k // 8, k % 8
                    g = h // 4
                    t_.wait_ge(smPTE, k + 1)
                    if k >= 1:
                        t_.wait_ge(smOE, k)
                    for kt in range(5):
                        ins = t_.matmul(pO[:, 0:128], V[:, q + kt, g * 128:(g + 1) * 128],
                                        pT_sb[k % 3][:, kt * 128:(kt + 1) * 128],
                                        start=(kt == 0), stop=(kt == 4))
                        if kt == 4:
                            ins.then_inc(smO, 1)

                def emit_d_slot(t_, k):
                    emit_s(t_, k)
                    if k >= 2:
                        emit_tr(t_, k - 2)
                    if k >= 3:
                        emit_o(t_, k - 3)

                def d_range(oc):
                    if oc >= 32:
                        return range(8 * 31, NK)
                    if oc >= 1:
                        return range(8 * (oc - 1), 8 * oc)
                    return range(0)

                with nc.Block("phCD", no_gpsimd_drain=True) as blk:
                    if "C" in phases:

                        @blk.sync
                        def _(sync):
                            sync.dma_start(out=amask_sb[:], in_=amask_pre[:]).then_inc(smDM, 16)

                        @blk.gpsimd
                        def _(g):
                            for H in range(64):
                                oc, hf = H // 2, H % 2
                                if H >= 2:
                                    if hf == 0:
                                        g.wait_ge(smQF, oc)
                                    else:
                                        g.wait_ge(smQT_, oc)
                                g.dma_start(
                                    out=wq_sb[H % 2][:],
                                    in_=wq_pre[oc].rearrange("p (j n) -> p j n", n=128)
                                    [:, hf * 16:(hf + 1) * 16, :]
                                ).then_inc(smQL[H % 2], 16)

                        @blk.tensor
                        def _(t_):
                            t_.wait_ge(smDM, 16)
                            for oc in range(33):
                                dks = list(d_range(oc))
                                if oc < 32:
                                    if oc >= 2:
                                        t_.wait_ge(smQE, oc - 1)
                                    for d0 in range(NDT):
                                        if d0 % 4 == 2 and dks:
                                            emit_d_slot(t_, dks.pop(0))
                                        emit_c_step(t_, oc, d0)
                                else:
                                    for k in dks:
                                        emit_d_slot(t_, k)
                            for j in (NK - 2, NK - 1):
                                emit_tr(t_, j)
                            for j in (NK - 3, NK - 2, NK - 1):
                                emit_o(t_, j)

                        @blk.scalar
                        def _(s):
                            for k in range(NK):
                                if k % 8 == 0:
                                    h = k // 8
                                    s.wait_ge(smQT_, h + 1)
                                    s.copy(qt_(h), pQ[h % 2][:]).then_inc(smQE, 1)
                                s.wait_ge(smS, k + 1)
                                if k >= 2:
                                    s.wait_ge(smSTT, k - 1)
                                s.activation(p_raw[k % 2][:], pS[:, 0:640], AF.Exp,
                                             scale=SCALE).then_inc(smEXP, 1)
                                if k >= 2:
                                    j = k - 2
                                    s.wait_ge(smTR, j + 1)
                                    if j >= 3:
                                        s.wait_ge(smO, j - 2)
                                    s.copy(pT_sb[j % 3][:], pT[:, 0:640]).then_inc(smPTE, 1)
                            for j in (NK - 2, NK - 1):
                                s.wait_ge(smTR, j + 1)
                                s.wait_ge(smO, j - 2)
                                s.copy(pT_sb[j % 3][:], pT[:, 0:640]).then_inc(smPTE, 1)

                        @blk.vector
                        def _(v):
                            for k in range(NK):
                                q = k % 8
                                v.wait_ge(smEXP, k + 1)
                                if k >= 3:
                                    v.wait_ge(smTR, k - 2)
                                v.scalar_tensor_tensor(out=p_msk[k % 2][:], in0=p_raw[k % 2][:],
                                                       scalar=1.0, in1=amask_sb[:, min(q, 4), :],
                                                       op0=ALU.mult, op1=ALU.mult,
                                                       accum_out=rowsum[k % 2][:]).then_inc(smSTT, 1)
                                v.wait_ge(smSTT, k + 1)
                                v.reciprocal(rcp[k % 2][:], rowsum[k % 2][:]).then_inc(smRC, 1)
                                v.wait_ge(smRC, k + 1)
                                v.tensor_scalar(out=p_n[k % 3][:], in0=p_msk[k % 2][:],
                                                scalar1=rcp[k % 2][:], scalar2=None,
                                                op0=ALU.mult).then_inc(smPN, 1)
                                if k >= 3:
                                    j = k - 3
                                    h2, q2 = j // 8, j % 8
                                    v.wait_ge(smO, j + 1)
                                    v.tensor_copy(qt_(h2)[:, q2 * 128:(q2 + 1) * 128],
                                                  pO[:, 0:128]).then_inc(smOE, 1)
                            for j in (NK - 3, NK - 2, NK - 1):
                                h2, q2 = j // 8, j % 8
                                v.wait_ge(smO, j + 1)
                                v.tensor_copy(qt_(h2)[:, q2 * 128:(q2 + 1) * 128],
                                              pO[:, 0:128]).then_inc(smOE, 1)

            if d_oT is not None:
                with nc.Block("dbgD", no_gpsimd_drain=True) as blk:
                    @blk.sync
                    def _(sync):
                        sync.dma_start(out=d_oT, in_=arenaB[:]).then_inc(smDBG, 16)
                        _dbgc[0] += 16
                        sync.wait_ge(smDBG, _dbgc[0])

        with contextlib.ExitStack() as stE:
            bflat_sb = stE.enter_context(nc.sbuf_tensor("bflat_sb", [128, D], BF16))
            wo_sb = [stE.enter_context(nc.sbuf_tensor(f"wo_sb{i}", [128, 32, 128], BF16))
                     for i in range(2)]
            xT_sb = [stE.enter_context(nc.sbuf_tensor(f"xT_sb{i}", [128, 1024], F32))
                     for i in range(2)]
            outT = [stE.enter_context(nc.sbuf_tensor(f"outT{i}", [128, 1024], F32))
                    for i in range(2)]
            pE = [stE.enter_context(nc.psum_tensor(f"pE{i}", [128, 1024], F32)) for i in range(2)]

            with nc.Block("phE", no_gpsimd_drain=True) as blk:
                if "E" in phases:

                    @blk.gpsimd
                    def _(g):
                        for dc in range(32):
                            if dc >= 2:
                                g.wait_ge(smET, dc - 1)
                            g.dma_start(out=wo_sb[dc % 2][:],
                                        in_=wo_pre[dc].rearrange("p (j n) -> p j n", n=128)
                                        ).then_inc(smWL[dc % 2], 16)
                            if dc == 0:
                                g.dma_start(out=bflat_sb[:], in_=bflat[:]).then_inc(smBF, 16)

                    @blk.sync
                    def _(sync):
                        for dc in range(32):
                            if dc >= 2:
                                sync.wait_ge(smEA, dc - 1)
                            sync.dma_start(out=xT_sb[dc % 2][:],
                                           in_=xT_core[dc * 128:(dc + 1) * 128, :]).then_inc(smXT[dc % 2], 16)
                            if dc >= 1:
                                sync.wait_ge(smEA, dc)
                                sync.dma_start(out=out[(dc - 1) * 128:dc * 128, :],
                                               in_=outT[(dc - 1) % 2][:]).then_inc(smOUT[(dc - 1) % 2], 16)
                        sync.wait_ge(smEA, 32)
                        sync.dma_start(out=out[31 * 128:32 * 128, :],
                                       in_=outT[31 % 2][:]).then_inc(smOUT[31 % 2], 16)
                        sync.wait_ge(smOUT[0], 16 * 16)
                        sync.wait_ge(smOUT[1], 16 * 16)

                    @blk.tensor
                    def _(t_):
                        for dc in range(32):
                            if dc == 0:
                                t_.wait_ge(smBF, 16)
                            t_.wait_ge(smWL[dc % 2], 16 * (dc // 2 + 1))
                            if dc >= 2:
                                t_.wait_ge(smEA, dc - 1)
                            for hc in range(32):
                                for half in range(2):
                                    t_.matmul(pE[dc % 2][:, half * 512:(half + 1) * 512],
                                              wo_sb[dc % 2][:, hc, :],
                                              qt_(hc)[:, half * 512:(half + 1) * 512],
                                              start=(hc == 0), stop=False)
                            for half in range(2):
                                ins = t_.matmul(pE[dc % 2][:, half * 512:(half + 1) * 512],
                                                bflat_sb[:, dc * 128:(dc + 1) * 128],
                                                whT[:, half * 512:(half + 1) * 512],
                                                start=False, stop=True)
                                if half == 1:
                                    ins.then_inc(smET, 1)

                    @blk.vector
                    def _(v):
                        for dc in range(32):
                            v.wait_ge(smET, dc + 1)
                            v.wait_ge(smXT[dc % 2], 16 * (dc // 2 + 1))
                            if dc >= 2:
                                v.wait_ge(smOUT[dc % 2], 16 * (dc // 2))
                            v.tensor_tensor(out=outT[dc % 2][:], in0=pE[dc % 2][:],
                                            in1=xT_sb[dc % 2][:], op=ALU.add).then_inc(smEA, 1)

    return nc, dbg


_CACHE = {}


def _get_program(dump=()):
    key = tuple(sorted(dump))
    if key not in _CACHE:
        _CACHE[key] = build_program(dump)
    return _CACHE[key]


def make_core_inputs(L0_attn_x, L0_attn_wq, L0_attn_wk, L0_attn_wv, L0_attn_wo,
                     L0_ffn_route_w, L0_ffn_A, L0_ffn_Bw):
    f32 = np.float32
    bf16 = ml_dtypes.bfloat16
    x = np.asarray(L0_attn_x, dtype=f32)
    wq = np.asarray(L0_attn_wq, dtype=f32)
    wk = np.asarray(L0_attn_wk, dtype=f32)
    wv = np.asarray(L0_attn_wv, dtype=f32)
    wo = np.asarray(L0_attn_wo, dtype=f32)
    route_w = np.asarray(L0_ffn_route_w, dtype=f32)
    A = np.asarray(L0_ffn_A, dtype=f32)
    Bw = np.asarray(L0_ffn_Bw, dtype=f32)

    wqT = wq.T
    wkT = wk.T
    wvT = wv.T
    woT = wo.T
    aT = A.reshape(128, D).T
    routeT = route_w.T
    shared = {
        "wq_pre": np.ascontiguousarray(
            wqT.reshape(32, 128, 32, 128).transpose(2, 1, 0, 3).reshape(32, 128, D)),
        "wk_pre": np.ascontiguousarray(
            wkT.reshape(32, 128, 8, 128).transpose(2, 1, 0, 3).reshape(8, 128, D)),
        "wv_pre": np.ascontiguousarray(
            wvT.reshape(32, 128, 8, 128).transpose(2, 1, 0, 3).reshape(8, 128, D)),
        "wo_pre": np.ascontiguousarray(
            woT.reshape(32, 128, 32, 128).transpose(2, 1, 0, 3).reshape(32, 128, D)),
        "aT_pre": np.ascontiguousarray(
            aT.reshape(32, 128, 128).transpose(1, 0, 2).reshape(128, D)),
        "routeT_pre": np.ascontiguousarray(
            routeT.reshape(32, 128, 8).transpose(1, 0, 2).reshape(128, 256)),
        "bflat": np.ascontiguousarray(Bw.transpose(0, 2, 1).reshape(128, D)),
        "emat": np.ascontiguousarray(
            np.repeat(np.eye(8, dtype=f32), 16, axis=1).reshape(8, 128)),
        "id_bf": np.eye(128, dtype=f32).astype(bf16),
        "id_f32": np.eye(128, dtype=f32),
        "id8": np.eye(8, dtype=f32),
    }

    qi = np.arange(128)[:, None]
    col = np.arange(640)[None, :]
    in_maps = []
    for c in range(8):
        b, q = c // 4, c % 4
        xe = np.zeros((TEXT, D), dtype=f32)
        if q > 0:
            xe[0:512] = x[b, q * 1024 - 512:q * 1024]
        xe[512:] = x[b, q * 1024:(q + 1) * 1024]
        xT = np.ascontiguousarray(x[b, q * 1024:(q + 1) * 1024].T)
        am = np.zeros((5, 128, 640), dtype=f32)
        for s in range(5):
            qt = s if s < 4 else 4
            allowed = (col > qi) & (col <= qi + 512)
            if s < 4:
                kpos0 = q * 1024 + qt * 128 - 512
                allowed = allowed & (kpos0 + col >= 0)
            am[s] = allowed.astype(f32)
        m = dict(shared)
        m["x_ext"] = xe
        m["xT_core"] = xT
        m["amask_pre"] = np.ascontiguousarray(am.transpose(1, 0, 2)).astype(bf16)
        in_maps.append(m)
    return in_maps


def kernel(L0_attn_x, L0_attn_wq, L0_attn_wk, L0_attn_wv, L0_attn_wo,
           L0_ffn_route_w, L0_ffn_A, L0_ffn_Bw):
    nc, _ = _get_program()
    in_maps = make_core_inputs(L0_attn_x, L0_attn_wq, L0_attn_wk, L0_attn_wv,
                               L0_attn_wo, L0_ffn_route_w, L0_ffn_A, L0_ffn_Bw)
    res = run_bass_kernel_spmd(nc, in_maps, core_ids=list(range(8)))
    out = np.empty((2, 4096, D), dtype=np.float32)
    for c in range(8):
        b, q = c // 4, c % 4
        out[b, q * 1024:(q + 1) * 1024, :] = np.asarray(res.results[c]["out"]).T
    return out


# version 19
# speedup vs baseline: 1.0611x; 1.0611x over previous
import contextlib
import numpy as np
import ml_dtypes

import concourse.bass as bass
import concourse.mybir as mybir
from concourse.bass_utils import run_bass_kernel_spmd

F32 = mybir.dt.float32
BF16 = mybir.dt.bfloat16
AF = mybir.ActivationFunctionType
ALU = mybir.AluOpType
AX = mybir.AxisListType

D = 4096
NDT = 32
TCORE = 1024
TEXT = 1536
NTILE = 12
NH = 32
NKV = 8
HD = 128
SCALE = 1.0 / float(np.sqrt(128.0))
NK = 256


def build_program(dump=(), phases=("A","A2","B","C","D","E"), a2s=5):
    nc = bass.Bass("TRN2", target_bir_lowering=False)

    x_ext = nc.dram_tensor("x_ext", [TEXT, D], F32, kind="ExternalInput").ap()
    xT_core = nc.dram_tensor("xT_core", [D, TCORE], F32, kind="ExternalInput").ap()
    wq_pre = nc.dram_tensor("wq_pre", [32, 128, D], F32, kind="ExternalInput").ap()
    wk_pre = nc.dram_tensor("wk_pre", [8, 128, D], F32, kind="ExternalInput").ap()
    wv_pre = nc.dram_tensor("wv_pre", [8, 128, D], F32, kind="ExternalInput").ap()
    wo_pre = nc.dram_tensor("wo_pre", [32, 128, D], F32, kind="ExternalInput").ap()
    aT_pre = nc.dram_tensor("aT_pre", [128, D], F32, kind="ExternalInput").ap()
    bflat = nc.dram_tensor("bflat", [128, D], F32, kind="ExternalInput").ap()
    routeT_pre = nc.dram_tensor("routeT_pre", [128, 256], F32, kind="ExternalInput").ap()
    emat = nc.dram_tensor("emat", [8, 128], F32, kind="ExternalInput").ap()
    id_bf = nc.dram_tensor("id_bf", [128, 128], BF16, kind="ExternalInput").ap()
    id_f32 = nc.dram_tensor("id_f32", [128, 128], F32, kind="ExternalInput").ap()
    id8 = nc.dram_tensor("id8", [8, 8], F32, kind="ExternalInput").ap()
    amask_pre = nc.dram_tensor("amask_pre", [128, 5, 640], BF16, kind="ExternalInput").ap()
    out = nc.dram_tensor("out", [D, TCORE], F32, kind="ExternalOutput").ap()

    dbg = {}
    def dbg_out(name, shape, dtype=F32):
        if name in dump:
            dbg[name] = nc.dram_tensor("dbg_" + name, shape, dtype, kind="ExternalOutput").ap()
            return dbg[name]
        return None

    d_xnT = dbg_out("xnT", [128, 32 * 1024], BF16)
    d_xnE = dbg_out("xnE", [128, 16 * 1024], BF16)
    d_whT = dbg_out("whT", [128, 1024], BF16)
    d_KT = dbg_out("KT", [128, 8, TEXT], BF16)
    d_V = dbg_out("V", [128, 12, 1024], BF16)
    d_QT = dbg_out("QT", [128, 32 * 1024], BF16)
    d_oT = dbg_out("oT", [128, 32 * 1024], BF16)

    with contextlib.ExitStack() as stP:
        def sb(name, shape, dtype, side=None):
            return stP.enter_context(nc.sbuf_tensor(name, shape, dtype, side=side))

        def sems(*names):
            return [stP.enter_context(nc.semaphore(n)) for n in names]

        arenaA = sb("arenaA", [128, 32 * 1024], BF16, side="right")
        arenaB = sb("arenaB", [128, 32 * 1024], BF16)

        def xnc(d0):
            return arenaA[:, d0 * 1024:(d0 + 1) * 1024]
        def xne(d0):
            return arenaB[:, d0 * 512:(d0 + 1) * 512]
        def qt_(h):
            return arenaB[:, h * 1024:(h + 1) * 1024]

        whT = sb("whT", [128, 1024], BF16)
        s_idbf = sb("s_idbf", [128, 128], BF16)
        s_idf = sb("s_idf", [128, 128], F32)
        s_id8 = sb("s_id8", [8, 8], F32)
        s_emat = sb("s_emat", [8, 128], F32)
        s_eps = sb("s_eps", [128, 1], F32)

        (smCL, smINIT, smVA, smVB, smSA, smVC, smVD, smSB, smTT, smVE) = sems(
            "smCL", "smINIT", "smVA", "smVB", "smSA", "smVC", "smVD", "smSB", "smTT", "smVE")
        smXL = sems("smXL0", "smXL1")
        smSE, = sems("smSE")
        (smGL, smT2, smV2) = sems("smGL", "smT2", "smV2")
        (smKT_, smKE, smKF, smVS, smTV, smVEv) = sems(
            "smKT", "smKE", "smKF", "smVS", "smTV", "smVEv")
        smKL = sems("smKL0", "smKL1")
        (smQT_, smQE, smQF) = sems("smQT", "smQE", "smQF")
        smQL = sems("smQL0", "smQL1")
        (smDM, smS, smEXP, smSTT, smRC, smPN, smTR, smPTE, smO, smOE) = sems(
            "smDM", "smS", "smEXP", "smSTT", "smRC", "smPN", "smTR", "smPTE", "smO", "smOE")
        (smWF, smET, smEA, smBF) = sems("smWF", "smET", "smEA", "smBF")
        smWL = sems("smWL0", "smWL1")
        smXT = sems("smXT0", "smXT1")
        smOUT = sems("smOUT0", "smOUT1")
        smDBG, = sems("smDBG")
        _dbgc = [0]

        with contextlib.ExitStack() as stA:
            xb = [stA.enter_context(nc.sbuf_tensor(f"xb{i}", [128, D], F32)) for i in range(2)]
            xn_tok = [stA.enter_context(nc.sbuf_tensor(f"xn_tok{i}", [128, D], BF16))
                      for i in range(2)]
            bnst = [stA.enter_context(nc.sbuf_tensor(f"bnst{i}", [128, 48], F32)) for i in range(2)]
            bnag = [stA.enter_context(nc.sbuf_tensor(f"bnag{i}", [128, 2], F32)) for i in range(2)]
            stds = [stA.enter_context(nc.sbuf_tensor(f"stds{i}", [128, 1], F32)) for i in range(2)]
            rstd = [stA.enter_context(nc.sbuf_tensor(f"rstd{i}", [128, 1], F32)) for i in range(2)]
            negmr = [stA.enter_context(nc.sbuf_tensor(f"negmr{i}", [128, 1], F32)) for i in range(2)]
            pbank = [stA.enter_context(nc.psum_tensor(f"pbank{i}", [128, 1024], BF16)) for i in range(2)]

            def xnT_dest(u, g):
                if u < 4:
                    buf, width, tcol = arenaB, 512, u * 128
                else:
                    buf, width, tcol = arenaA, 1024, (u - 4) * 128
                full = buf[:, 0:32 * width].rearrange("p (a w) -> p a w", w=width)
                return full[:, 8 * g:8 * g + 8, tcol:tcol + 128]

            with nc.Block("phA", no_gpsimd_drain=True) as blk:
                if "A" in phases:

                    @blk.sync
                    def _(sync):
                        sync.dma_start(out=s_idbf[:], in_=id_bf[:]).then_inc(smCL, 16)
                        sync.dma_start(out=s_idf[:], in_=id_f32[:]).then_inc(smCL, 16)
                        sync.dma_start(out=s_id8[:], in_=id8[:]).then_inc(smCL, 16)
                        sync.dma_start(out=s_emat[:], in_=emat[:]).then_inc(smCL, 16)
                        for t in range(NTILE):
                            if t >= 2:
                                sync.wait_ge(smVB, t - 1)
                                sync.wait_ge(smSB, t - 1)
                            sync.dma_start(out=xb[t % 2][:],
                                           in_=x_ext[t * 128:(t + 1) * 128, :]).then_inc(smXL[t % 2], 16)

                    @blk.vector
                    def _(v):
                        v.memset(s_eps[:], 1e-5).then_inc(smINIT, 1)
                        for t in range(NTILE):
                            v.wait_ge(smXL[t % 2], 16 * (t // 2 + 1))
                            for c in range(8):
                                ins = v.bn_stats(bnst[t % 2][:, c * 6:(c + 1) * 6],
                                                 xb[t % 2][:, c * 512:(c + 1) * 512])
                                if c == 7:
                                    ins.then_inc(smVA, 1)
                            v.wait_ge(smVA, t + 1)
                            v.bn_aggr(bnag[t % 2][:], bnst[t % 2][:]).then_inc(smVB, 1)
                            v.wait_ge(smSA, t + 1)
                            v.reciprocal(rstd[t % 2][:], stds[t % 2][:]).then_inc(smVC, 1)
                            v.wait_ge(smVC, t + 1)
                            v.scalar_tensor_tensor(out=negmr[t % 2][:], in0=bnag[t % 2][:, 0:1],
                                                   scalar=-1.0, in1=rstd[t % 2][:],
                                                   op0=ALU.mult, op1=ALU.mult).then_inc(smVD, 1)
                            if t >= 1:
                                for g in range(0, 4, 2):
                                    G = 4 * (t - 1) + g
                                    v.wait_ge(smTT, G + 1)
                                    v.tensor_copy(
                                        xnT_dest(t - 1, g),
                                        pbank[G % 2][:].rearrange("p (a w) -> p a w", w=128)
                                    ).then_inc(smVE, 1)
                        for g in range(0, 4, 2):
                            G = 44 + g
                            v.wait_ge(smTT, G + 1)
                            v.tensor_copy(
                                xnT_dest(11, g),
                                pbank[G % 2][:].rearrange("p (a w) -> p a w", w=128)
                            ).then_inc(smVE, 1)

                    @blk.scalar
                    def _(s):
                        s.wait_ge(smINIT, 1)
                        for t in range(NTILE):
                            s.wait_ge(smVB, t + 1)
                            s.activation(stds[t % 2][:], bnag[t % 2][:, 1:2], AF.Sqrt,
                                         bias=s_eps[:]).then_inc(smSA, 1)
                            s.wait_ge(smVD, t + 1)
                            if t >= 2:
                                s.wait_ge(smTT, 4 * (t - 1))
                            s.activation(xn_tok[t % 2][:], xb[t % 2][:], AF.Identity,
                                         bias=negmr[t % 2][:],
                                         scale=rstd[t % 2][:]).then_inc(smSB, 1)
                            if t >= 1:
                                for g in range(1, 4, 2):
                                    G = 4 * (t - 1) + g
                                    s.wait_ge(smTT, G + 1)
                                    s.copy(
                                        xnT_dest(t - 1, g),
                                        pbank[G % 2][:].rearrange("p (a w) -> p a w", w=128)
                                    ).then_inc(smSE, 1)
                        for g in range(1, 4, 2):
                            G = 44 + g
                            s.wait_ge(smTT, G + 1)
                            s.copy(
                                xnT_dest(11, g),
                                pbank[G % 2][:].rearrange("p (a w) -> p a w", w=128)
                            ).then_inc(smSE, 1)

                    @blk.tensor
                    def _(t_):
                        t_.wait_ge(smCL, 64)
                        for t in range(NTILE):
                            t_.wait_ge(smSB, t + 1)
                            for g in range(4):
                                G = 4 * t + g
                                if G >= 2:
                                    Gp = G - 2
                                    if Gp % 2 == 0:
                                        t_.wait_ge(smVE, Gp // 2 + 1)
                                    else:
                                        t_.wait_ge(smSE, Gp // 2 + 1)
                                for j in range(8):
                                    ins = t_.transpose(
                                        pbank[G % 2][:, j * 128:(j + 1) * 128],
                                        xn_tok[t % 2][:, (8 * g + j) * 128:(8 * g + j + 1) * 128],
                                        s_idbf[:])
                                    if j == 7:
                                        ins.then_inc(smTT, 1)

            if d_xnT is not None:
                with nc.Block("dbgA", no_gpsimd_drain=True) as blk:
                    @blk.sync
                    def _(sync):
                        sync.dma_start(out=d_xnT, in_=arenaA[:]).then_inc(smDBG, 16)
                        sync.dma_start(out=d_xnE, in_=arenaB[:, 0:16384]).then_inc(smDBG, 16)
                        _dbgc[0] += 32
                        sync.wait_ge(smDBG, _dbgc[0])

        with contextlib.ExitStack() as stA2:
            aT_sb = stA2.enter_context(nc.sbuf_tensor("aT_sb", [128, 32, 128], BF16))
            routeT_sb = stA2.enter_context(nc.sbuf_tensor("routeT_sb", [128, 32, 8], BF16))
            lgT_sb = stA2.enter_context(nc.sbuf_tensor("lgT_sb", [8, 1024], F32))
            lg_tok = stA2.enter_context(nc.sbuf_tensor("lg_tok", [128, 64], F32))
            maxv = stA2.enter_context(nc.sbuf_tensor("maxv", [128, 8], F32))
            eq = stA2.enter_context(nc.sbuf_tensor("eq", [128, 64], F32))
            cnt = stA2.enter_context(nc.sbuf_tensor("cnt", [128, 8], F32))
            rc_ = stA2.enter_context(nc.sbuf_tensor("rc_", [128, 8], F32))
            rcm = stA2.enter_context(nc.sbuf_tensor("rcm", [128, 8], F32))
            rw_tok = stA2.enter_context(nc.sbuf_tensor("rw_tok", [128, 64], F32))
            rwT_sb = stA2.enter_context(nc.sbuf_tensor("rwT_sb", [8, 1024], F32))
            rw_er = stA2.enter_context(nc.sbuf_tensor("rw_er", [128, 1024], F32))
            pRW8 = stA2.enter_context(nc.psum_tensor("pRW8", [8, 1024], F32))
            pSm = stA2.enter_context(nc.psum_tensor("pSm", [128, 512], F32))
            pBig1 = stA2.enter_context(nc.psum_tensor("pBig1", [128, 1024], F32))
            pBig2 = stA2.enter_context(nc.psum_tensor("pBig2", [128, 1024], F32))

            with nc.Block("phA2", no_gpsimd_drain=True) as blk:
                if "A2" in phases:

                    @blk.gpsimd
                    def _(g):
                        g.dma_start(out=aT_sb[:],
                                    in_=aT_pre.rearrange("p (j n) -> p j n", n=128)).then_inc(smGL, 16)
                        g.dma_start(out=routeT_sb[:],
                                    in_=routeT_pre.rearrange("p (j n) -> p j n", n=8)).then_inc(smGL, 16)

                    @blk.tensor
                    def _(t_):
                        t_.wait_ge(smCL, 64)
                        t_.wait_ge(smGL, 32)
                        for d0 in range(NDT):
                            for half in range(2):
                                ins = t_.matmul(pRW8[:, half * 512:(half + 1) * 512],
                                                routeT_sb[:, d0, :],
                                                xnc(d0)[:, half * 512:(half + 1) * 512],
                                                start=(d0 == 0), stop=(d0 == NDT - 1))
                                if d0 == NDT - 1 and half == 1:
                                    ins.then_inc(smT2, 1)
                        if a2s >= 2:
                            t_.wait_ge(smV2, 1)
                            for tt in range(8):
                                ins = t_.transpose(pSm[:, tt * 8:(tt + 1) * 8],
                                                   lgT_sb[:, tt * 128:(tt + 1) * 128], s_id8[:])
                                if tt == 7:
                                    ins.then_inc(smT2, 1)
                        if a2s >= 3:
                            t_.wait_ge(smV2, 8)
                            for tt in range(8):
                                ins = t_.transpose(pRW8[0:8, tt * 128:(tt + 1) * 128],
                                                   rw_tok[:, tt * 8:(tt + 1) * 8], s_idf[:])
                                if tt == 7:
                                    ins.then_inc(smT2, 1)
                            t_.wait_ge(smV2, 9)
                            for half in range(2):
                                ins = t_.matmul(pBig1[:, half * 512:(half + 1) * 512], s_emat[:],
                                                rwT_sb[:, half * 512:(half + 1) * 512],
                                                start=True, stop=True)
                                if half == 1:
                                    ins.then_inc(smT2, 1)
                        if a2s >= 4:
                            for d0 in range(NDT):
                                for half in range(2):
                                    ins = t_.matmul(pBig2[:, half * 512:(half + 1) * 512],
                                                    aT_sb[:, d0, :],
                                                    xnc(d0)[:, half * 512:(half + 1) * 512],
                                                    start=(d0 == 0), stop=(d0 == NDT - 1))
                                    if d0 == NDT - 1 and half == 1:
                                        ins.then_inc(smT2, 1)

                    @blk.vector
                    def _(v):
                        v.wait_ge(smT2, 1)
                        v.tensor_copy(lgT_sb[:], pRW8[:]).then_inc(smV2, 1)
                        if a2s < 2:
                            return
                        v.wait_ge(smT2, 2)
                        v.tensor_copy(lg_tok[:], pSm[:, 0:64]).then_inc(smV2, 1)
                        v.wait_ge(smV2, 2)
                        lg3 = lg_tok[:].rearrange("p (g e) -> p g e", e=8)
                        v.tensor_reduce(out=maxv[:].rearrange("p (g e) -> p g e", e=1),
                                        in_=lg3, axis=AX.X, op=ALU.max).then_inc(smV2, 1)
                        v.wait_ge(smV2, 3)
                        mx_b = maxv[:].rearrange("p (g e) -> p g e", e=1).broadcast_to((128, 8, 8))
                        v.tensor_tensor(out=eq[:].rearrange("p (g e) -> p g e", e=8),
                                        in0=lg3, in1=mx_b, op=ALU.is_equal).then_inc(smV2, 1)
                        v.wait_ge(smV2, 4)
                        v.tensor_reduce(out=cnt[:].rearrange("p (g e) -> p g e", e=1),
                                        in_=eq[:].rearrange("p (g e) -> p g e", e=8),
                                        axis=AX.X, op=ALU.add).then_inc(smV2, 1)
                        v.wait_ge(smV2, 5)
                        v.reciprocal(rc_[:], cnt[:]).then_inc(smV2, 1)
                        v.wait_ge(smV2, 6)
                        v.tensor_tensor(out=rcm[:], in0=rc_[:], in1=maxv[:],
                                        op=ALU.mult).then_inc(smV2, 1)
                        v.wait_ge(smV2, 7)
                        rc_b = rcm[:].rearrange("p (g e) -> p g e", e=1).broadcast_to((128, 8, 8))
                        v.tensor_tensor(out=rw_tok[:].rearrange("p (g e) -> p g e", e=8),
                                        in0=eq[:].rearrange("p (g e) -> p g e", e=8),
                                        in1=rc_b, op=ALU.mult).then_inc(smV2, 1)
                        if a2s < 3:
                            return
                        v.wait_ge(smT2, 3)
                        v.tensor_copy(rwT_sb[:], pRW8[0:8, :]).then_inc(smV2, 1)
                        v.wait_ge(smT2, 4)
                        v.tensor_copy(rw_er[:], pBig1[:]).then_inc(smV2, 1)
                        if a2s < 4:
                            return
                        v.wait_ge(smT2, 5)
                        v.wait_ge(smV2, 10)
                        v.tensor_tensor(out=whT[:], in0=pBig2[:], in1=rw_er[:],
                                        op=ALU.mult).then_inc(smV2, 1)

            if d_whT is not None:
                with nc.Block("dbgA2", no_gpsimd_drain=True) as blk:
                    @blk.sync
                    def _(sync):
                        sync.dma_start(out=d_whT, in_=whT[:]).then_inc(smDBG, 16)
                        _dbgc[0] += 16
                        sync.wait_ge(smDBG, _dbgc[0])

        with contextlib.ExitStack() as stKV:
            KT = stKV.enter_context(
                nc.sbuf_tensor("KT", [128, 8, TEXT], BF16, side="right"))
            V = stKV.enter_context(
                nc.sbuf_tensor("V", [128, 12, 1024], BF16, side="right"))
            wq_sb = [stKV.enter_context(
                nc.sbuf_tensor(f"wq_sb{i}", [128, 16, 128], BF16)) for i in range(2)]
            amask_sb = stKV.enter_context(
                nc.sbuf_tensor("amask_sb", [128, 5, 640], BF16))
            with contextlib.ExitStack() as stB:
                wk_sb = [stB.enter_context(nc.sbuf_tensor(f"wk_sb{i}", [128, 16, 128], BF16))
                         for i in range(2)]
                VT_stage = stB.enter_context(nc.sbuf_tensor("VT_stage", [128, TEXT], BF16))
                pK = [stB.enter_context(nc.psum_tensor(f"pK{i}", [128, 1536], F32)) for i in range(2)]
                pTV = stB.enter_context(nc.psum_tensor("pTV", [128, 1536], BF16))

                def xn_rhs(d0, part):
                    if part == 0:
                        return xne(d0)
                    if part == 1:
                        return xnc(d0)[:, 0:512]
                    return xnc(d0)[:, 512:1024]

                with nc.Block("phB", no_gpsimd_drain=True) as blk:
                    if "B" in phases:

                        @blk.sync
                        def _(sync):
                            sync.dma_start(out=amask_sb[:], in_=amask_pre[:]).then_inc(smDM, 16)

                        @blk.gpsimd
                        def _(g):
                            for H in range(32):
                                p, hf = H // 2, H % 2
                                w_pre = wk_pre[p] if p < 8 else wv_pre[p - 8]
                                if H >= 2:
                                    if hf == 0:
                                        g.wait_ge(smKF, p)
                                    else:
                                        g.wait_ge(smKT_, p)
                                g.dma_start(
                                    out=wk_sb[H % 2][:],
                                    in_=w_pre.rearrange("p (j n) -> p j n", n=128)
                                    [:, hf * 16:(hf + 1) * 16, :]
                                ).then_inc(smKL[H % 2], 16)
                            for H in range(2):
                                g.dma_start(
                                    out=wq_sb[H % 2][:],
                                    in_=wq_pre[0].rearrange("p (j n) -> p j n", n=128)
                                    [:, (H % 2) * 16:(H % 2 + 1) * 16, :]
                                ).then_inc(smQL[H % 2], 16)

                        @blk.tensor
                        def _(t_):
                            for p in range(16):
                                oc = p if p < 8 else p - 8
                                if p >= 2:
                                    t_.wait_ge(smVS, p - 1)
                                for d0 in range(NDT):
                                    H = 2 * p + d0 // 16
                                    if d0 % 16 == 0:
                                        t_.wait_ge(smKL[H % 2], 16 * (H // 2 + 1))
                                    for part in range(3):
                                        ins = t_.matmul(pK[p % 2][:, part * 512:(part + 1) * 512],
                                                        wk_sb[H % 2][:, d0 % 16, :],
                                                        xn_rhs(d0, part),
                                                        start=(d0 == 0), stop=(d0 == NDT - 1))
                                        if d0 == 15 and part == 2:
                                            ins.then_inc(smKF, 1)
                                        if d0 == NDT - 1 and part == 2:
                                            ins.then_inc(smKT_, 1)
                                if p >= 8:
                                    oc_v = p - 8 - 1
                                    if oc_v >= 0:
                                        t_.wait_ge(smVS, p)
                                        if oc_v >= 1:
                                            t_.wait_ge(smVEv, oc_v)
                                        for jt in range(12):
                                            ins = t_.transpose(
                                                pTV[:, jt * 128:(jt + 1) * 128],
                                                VT_stage[:, jt * 128:(jt + 1) * 128],
                                                s_idbf[:])
                                            if jt == 11:
                                                ins.then_inc(smTV, 1)
                            t_.wait_ge(smVS, 16)
                            t_.wait_ge(smVEv, 7)
                            for jt in range(12):
                                ins = t_.transpose(pTV[:, jt * 128:(jt + 1) * 128],
                                                   VT_stage[:, jt * 128:(jt + 1) * 128],
                                                   s_idbf[:])
                                if jt == 11:
                                    ins.then_inc(smTV, 1)

                        @blk.vector
                        def _(v):
                            for p in range(16):
                                v.wait_ge(smKT_, p + 1)
                                if p < 8:
                                    v.tensor_copy(KT[:, p, :], pK[p % 2][:]).then_inc(smVS, 1)
                                else:
                                    oc_v = p - 8
                                    if oc_v >= 1:
                                        v.wait_ge(smTV, oc_v)
                                    v.tensor_copy(VT_stage[:], pK[p % 2][:]).then_inc(smVS, 1)
                                    if oc_v >= 1:
                                        v.tensor_copy(
                                            V[:].rearrange("p a w -> p (a w)")
                                            .rearrange("p (a w) -> p a w", w=1024)
                                            [:, :, (oc_v - 1) * 128:oc_v * 128],
                                            pTV[:].rearrange("p (a w) -> p a w", w=128)
                                        ).then_inc(smVEv, 1)
                            v.wait_ge(smTV, 8)
                            v.tensor_copy(
                                V[:].rearrange("p a w -> p (a w)")
                                .rearrange("p (a w) -> p a w", w=1024)
                                [:, :, 7 * 128:8 * 128],
                                pTV[:].rearrange("p (a w) -> p a w", w=128)
                            ).then_inc(smVEv, 1)

            with contextlib.ExitStack() as stCD:
                p_raw = [stCD.enter_context(nc.sbuf_tensor(f"p_raw{i}", [128, 640], BF16))
                         for i in range(2)]
                p_msk = [stCD.enter_context(nc.sbuf_tensor(f"p_msk{i}", [128, 640], BF16))
                         for i in range(2)]
                p_n = [stCD.enter_context(nc.sbuf_tensor(f"p_n{i}", [128, 640], BF16))
                       for i in range(3)]
                pT_sb = [stCD.enter_context(nc.sbuf_tensor(f"pT_sb{i}", [128, 640], BF16))
                         for i in range(3)]
                rowsum = [stCD.enter_context(nc.sbuf_tensor(f"rowsum{i}", [128, 1], F32))
                          for i in range(2)]
                rcp = [stCD.enter_context(nc.sbuf_tensor(f"rcp{i}", [128, 1], F32))
                       for i in range(2)]
                pQ = [stCD.enter_context(nc.psum_tensor(f"pQ{i}", [128, 1024], F32))
                      for i in range(2)]
                pS = stCD.enter_context(nc.psum_tensor("pS", [128, 1024], F32))
                pT = stCD.enter_context(nc.psum_tensor("pT", [128, 1024], BF16))
                pO = stCD.enter_context(nc.psum_tensor("pO", [128, 512], F32))

                def emit_c_step(t_, oc, d0):
                    H = 2 * oc + d0 // 16
                    if d0 % 16 == 0:
                        t_.wait_ge(smQL[H % 2], 16 * (H // 2 + 1))
                    for half in range(2):
                        ins = t_.matmul(pQ[oc % 2][:, half * 512:(half + 1) * 512],
                                        wq_sb[H % 2][:, d0 % 16, :],
                                        xnc(d0)[:, half * 512:(half + 1) * 512],
                                        start=(d0 == 0), stop=(d0 == NDT - 1))
                        if d0 == 15 and half == 1:
                            ins.then_inc(smQF, 1)
                        if d0 == NDT - 1 and half == 1:
                            ins.then_inc(smQT_, 1)

                def emit_s(t_, k):
                    h, q = k // 8, k % 8
                    g = h // 4
                    t_.wait_ge(smQE, h + 1)
                    if k >= 1:
                        t_.wait_ge(smEXP, k)
                    lhs = qt_(h)[:, q * 128:(q + 1) * 128]
                    t_.matmul(pS[:, 0:512], lhs,
                              KT[:, g, q * 128:q * 128 + 512], start=True, stop=True)
                    t_.matmul(pS[:, 512:640], lhs,
                              KT[:, g, q * 128 + 512:q * 128 + 640],
                              start=True, stop=True).then_inc(smS, 1)

                def emit_tr(t_, k):
                    if k >= 1:
                        t_.wait_ge(smPTE, k)
                    t_.wait_ge(smPN, k + 1)
                    for kt in range(5):
                        ins = t_.transpose(pT[:, kt * 128:(kt + 1) * 128],
                                           p_n[k % 3][:, kt * 128:(kt + 1) * 128], s_idbf[:])
                        if kt == 4:
                            ins.then_inc(smTR, 1)

                def emit_o(t_, k):
                    h, q = k // 8, k % 8
                    g = h // 4
                    t_.wait_ge(smPTE, k + 1)
                    if k >= 1:
                        t_.wait_ge(smOE, k)
                    for kt in range(5):
                        ins = t_.matmul(pO[:, 0:128], V[:, q + kt, g * 128:(g + 1) * 128],
                                        pT_sb[k % 3][:, kt * 128:(kt + 1) * 128],
                                        start=(kt == 0), stop=(kt == 4))
                        if kt == 4:
                            ins.then_inc(smO, 1)

                def emit_d_slot(t_, k):
                    emit_s(t_, k)
                    if k >= 2:
                        emit_tr(t_, k - 2)
                    if k >= 3:
                        emit_o(t_, k - 3)

                def d_range(oc):
                    if oc >= 32:
                        return range(8 * 31, NK)
                    if oc >= 1:
                        return range(8 * (oc - 1), 8 * oc)
                    return range(0)

                with nc.Block("phCD", no_gpsimd_drain=True) as blk:
                    if "C" in phases:

                        @blk.gpsimd
                        def _(g):
                            for H in range(2, 64):
                                oc, hf = H // 2, H % 2
                                if H >= 2:
                                    if hf == 0:
                                        g.wait_ge(smQF, oc)
                                    else:
                                        g.wait_ge(smQT_, oc)
                                g.dma_start(
                                    out=wq_sb[H % 2][:],
                                    in_=wq_pre[oc].rearrange("p (j n) -> p j n", n=128)
                                    [:, hf * 16:(hf + 1) * 16, :]
                                ).then_inc(smQL[H % 2], 16)

                        @blk.tensor
                        def _(t_):
                            t_.wait_ge(smDM, 16)
                            for oc in range(33):
                                dks = list(d_range(oc))
                                if oc < 32:
                                    if oc >= 2:
                                        t_.wait_ge(smQE, oc - 1)
                                    for d0 in range(NDT):
                                        if d0 % 4 == 2 and dks:
                                            emit_d_slot(t_, dks.pop(0))
                                        emit_c_step(t_, oc, d0)
                                else:
                                    for k in dks:
                                        emit_d_slot(t_, k)
                            for j in (NK - 2, NK - 1):
                                emit_tr(t_, j)
                            for j in (NK - 3, NK - 2, NK - 1):
                                emit_o(t_, j)

                        @blk.scalar
                        def _(s):
                            for k in range(NK):
                                if k % 8 == 0:
                                    h = k // 8
                                    s.wait_ge(smQT_, h + 1)
                                    s.copy(qt_(h), pQ[h % 2][:]).then_inc(smQE, 1)
                                s.wait_ge(smS, k + 1)
                                if k >= 2:
                                    s.wait_ge(smSTT, k - 1)
                                s.activation(p_raw[k % 2][:], pS[:, 0:640], AF.Exp,
                                             scale=SCALE).then_inc(smEXP, 1)
                                if k >= 2:
                                    j = k - 2
                                    s.wait_ge(smTR, j + 1)
                                    if j >= 3:
                                        s.wait_ge(smO, j - 2)
                                    s.copy(pT_sb[j % 3][:], pT[:, 0:640]).then_inc(smPTE, 1)
                            for j in (NK - 2, NK - 1):
                                s.wait_ge(smTR, j + 1)
                                s.wait_ge(smO, j - 2)
                                s.copy(pT_sb[j % 3][:], pT[:, 0:640]).then_inc(smPTE, 1)

                        @blk.vector
                        def _(v):
                            for k in range(NK):
                                q = k % 8
                                v.wait_ge(smEXP, k + 1)
                                if k >= 3:
                                    v.wait_ge(smTR, k - 2)
                                v.scalar_tensor_tensor(out=p_msk[k % 2][:], in0=p_raw[k % 2][:],
                                                       scalar=1.0, in1=amask_sb[:, min(q, 4), :],
                                                       op0=ALU.mult, op1=ALU.mult,
                                                       accum_out=rowsum[k % 2][:]).then_inc(smSTT, 1)
                                v.wait_ge(smSTT, k + 1)
                                v.reciprocal(rcp[k % 2][:], rowsum[k % 2][:]).then_inc(smRC, 1)
                                v.wait_ge(smRC, k + 1)
                                v.tensor_scalar(out=p_n[k % 3][:], in0=p_msk[k % 2][:],
                                                scalar1=rcp[k % 2][:], scalar2=None,
                                                op0=ALU.mult).then_inc(smPN, 1)
                                if k >= 3:
                                    j = k - 3
                                    h2, q2 = j // 8, j % 8
                                    v.wait_ge(smO, j + 1)
                                    v.tensor_copy(qt_(h2)[:, q2 * 128:(q2 + 1) * 128],
                                                  pO[:, 0:128]).then_inc(smOE, 1)
                            for j in (NK - 3, NK - 2, NK - 1):
                                h2, q2 = j // 8, j % 8
                                v.wait_ge(smO, j + 1)
                                v.tensor_copy(qt_(h2)[:, q2 * 128:(q2 + 1) * 128],
                                              pO[:, 0:128]).then_inc(smOE, 1)

            if d_oT is not None:
                with nc.Block("dbgD", no_gpsimd_drain=True) as blk:
                    @blk.sync
                    def _(sync):
                        sync.dma_start(out=d_oT, in_=arenaB[:]).then_inc(smDBG, 16)
                        _dbgc[0] += 16
                        sync.wait_ge(smDBG, _dbgc[0])

        with contextlib.ExitStack() as stE:
            bflat_sb = stE.enter_context(nc.sbuf_tensor("bflat_sb", [128, D], BF16))
            wo_sb = [stE.enter_context(nc.sbuf_tensor(f"wo_sb{i}", [128, 32, 128], BF16))
                     for i in range(2)]
            xT_sb = [stE.enter_context(nc.sbuf_tensor(f"xT_sb{i}", [128, 1024], F32))
                     for i in range(2)]
            outT = [stE.enter_context(nc.sbuf_tensor(f"outT{i}", [128, 1024], F32))
                    for i in range(2)]
            pE = [stE.enter_context(nc.psum_tensor(f"pE{i}", [128, 1024], F32)) for i in range(2)]

            with nc.Block("phE", no_gpsimd_drain=True) as blk:
                if "E" in phases:

                    @blk.gpsimd
                    def _(g):
                        for dc in range(32):
                            if dc >= 2:
                                g.wait_ge(smET, dc - 1)
                            g.dma_start(out=wo_sb[dc % 2][:],
                                        in_=wo_pre[dc].rearrange("p (j n) -> p j n", n=128)
                                        ).then_inc(smWL[dc % 2], 16)
                            if dc == 0:
                                g.dma_start(out=bflat_sb[:], in_=bflat[:]).then_inc(smBF, 16)

                    @blk.sync
                    def _(sync):
                        for dc in range(32):
                            if dc >= 2:
                                sync.wait_ge(smEA, dc - 1)
                            sync.dma_start(out=xT_sb[dc % 2][:],
                                           in_=xT_core[dc * 128:(dc + 1) * 128, :]).then_inc(smXT[dc % 2], 16)
                            if dc >= 1:
                                sync.wait_ge(smEA, dc)
                                sync.dma_start(out=out[(dc - 1) * 128:dc * 128, :],
                                               in_=outT[(dc - 1) % 2][:]).then_inc(smOUT[(dc - 1) % 2], 16)
                        sync.wait_ge(smEA, 32)
                        sync.dma_start(out=out[31 * 128:32 * 128, :],
                                       in_=outT[31 % 2][:]).then_inc(smOUT[31 % 2], 16)
                        sync.wait_ge(smOUT[0], 16 * 16)
                        sync.wait_ge(smOUT[1], 16 * 16)

                    @blk.tensor
                    def _(t_):
                        for dc in range(32):
                            if dc == 0:
                                t_.wait_ge(smBF, 16)
                            t_.wait_ge(smWL[dc % 2], 16 * (dc // 2 + 1))
                            if dc >= 2:
                                t_.wait_ge(smEA, dc - 1)
                            for hc in range(32):
                                for half in range(2):
                                    t_.matmul(pE[dc % 2][:, half * 512:(half + 1) * 512],
                                              wo_sb[dc % 2][:, hc, :],
                                              qt_(hc)[:, half * 512:(half + 1) * 512],
                                              start=(hc == 0), stop=False)
                            for half in range(2):
                                ins = t_.matmul(pE[dc % 2][:, half * 512:(half + 1) * 512],
                                                bflat_sb[:, dc * 128:(dc + 1) * 128],
                                                whT[:, half * 512:(half + 1) * 512],
                                                start=False, stop=True)
                                if half == 1:
                                    ins.then_inc(smET, 1)

                    @blk.vector
                    def _(v):
                        for dc in range(32):
                            v.wait_ge(smET, dc + 1)
                            v.wait_ge(smXT[dc % 2], 16 * (dc // 2 + 1))
                            if dc >= 2:
                                v.wait_ge(smOUT[dc % 2], 16 * (dc // 2))
                            v.tensor_tensor(out=outT[dc % 2][:], in0=pE[dc % 2][:],
                                            in1=xT_sb[dc % 2][:], op=ALU.add).then_inc(smEA, 1)

    return nc, dbg


_CACHE = {}


def _get_program(dump=()):
    key = tuple(sorted(dump))
    if key not in _CACHE:
        _CACHE[key] = build_program(dump)
    return _CACHE[key]


def make_core_inputs(L0_attn_x, L0_attn_wq, L0_attn_wk, L0_attn_wv, L0_attn_wo,
                     L0_ffn_route_w, L0_ffn_A, L0_ffn_Bw):
    f32 = np.float32
    bf16 = ml_dtypes.bfloat16
    x = np.asarray(L0_attn_x, dtype=f32)
    wq = np.asarray(L0_attn_wq, dtype=f32)
    wk = np.asarray(L0_attn_wk, dtype=f32)
    wv = np.asarray(L0_attn_wv, dtype=f32)
    wo = np.asarray(L0_attn_wo, dtype=f32)
    route_w = np.asarray(L0_ffn_route_w, dtype=f32)
    A = np.asarray(L0_ffn_A, dtype=f32)
    Bw = np.asarray(L0_ffn_Bw, dtype=f32)

    wqT = wq.T
    wkT = wk.T
    wvT = wv.T
    woT = wo.T
    aT = A.reshape(128, D).T
    routeT = route_w.T
    shared = {
        "wq_pre": np.ascontiguousarray(
            wqT.reshape(32, 128, 32, 128).transpose(2, 1, 0, 3).reshape(32, 128, D)),
        "wk_pre": np.ascontiguousarray(
            wkT.reshape(32, 128, 8, 128).transpose(2, 1, 0, 3).reshape(8, 128, D)),
        "wv_pre": np.ascontiguousarray(
            wvT.reshape(32, 128, 8, 128).transpose(2, 1, 0, 3).reshape(8, 128, D)),
        "wo_pre": np.ascontiguousarray(
            woT.reshape(32, 128, 32, 128).transpose(2, 1, 0, 3).reshape(32, 128, D)),
        "aT_pre": np.ascontiguousarray(
            aT.reshape(32, 128, 128).transpose(1, 0, 2).reshape(128, D)),
        "routeT_pre": np.ascontiguousarray(
            routeT.reshape(32, 128, 8).transpose(1, 0, 2).reshape(128, 256)),
        "bflat": np.ascontiguousarray(Bw.transpose(0, 2, 1).reshape(128, D)),
        "emat": np.ascontiguousarray(
            np.repeat(np.eye(8, dtype=f32), 16, axis=1).reshape(8, 128)),
        "id_bf": np.eye(128, dtype=f32).astype(bf16),
        "id_f32": np.eye(128, dtype=f32),
        "id8": np.eye(8, dtype=f32),
    }

    qi = np.arange(128)[:, None]
    col = np.arange(640)[None, :]
    in_maps = []
    for c in range(8):
        b, q = c // 4, c % 4
        xe = np.zeros((TEXT, D), dtype=f32)
        if q > 0:
            xe[0:512] = x[b, q * 1024 - 512:q * 1024]
        xe[512:] = x[b, q * 1024:(q + 1) * 1024]
        xT = np.ascontiguousarray(x[b, q * 1024:(q + 1) * 1024].T)
        am = np.zeros((5, 128, 640), dtype=f32)
        for s in range(5):
            qt = s if s < 4 else 4
            allowed = (col > qi) & (col <= qi + 512)
            if s < 4:
                kpos0 = q * 1024 + qt * 128 - 512
                allowed = allowed & (kpos0 + col >= 0)
            am[s] = allowed.astype(f32)
        m = dict(shared)
        m["x_ext"] = xe
        m["xT_core"] = xT
        m["amask_pre"] = np.ascontiguousarray(am.transpose(1, 0, 2)).astype(bf16)
        in_maps.append(m)
    return in_maps


def kernel(L0_attn_x, L0_attn_wq, L0_attn_wk, L0_attn_wv, L0_attn_wo,
           L0_ffn_route_w, L0_ffn_A, L0_ffn_Bw):
    nc, _ = _get_program()
    in_maps = make_core_inputs(L0_attn_x, L0_attn_wq, L0_attn_wk, L0_attn_wv,
                               L0_attn_wo, L0_ffn_route_w, L0_ffn_A, L0_ffn_Bw)
    res = run_bass_kernel_spmd(nc, in_maps, core_ids=list(range(8)))
    out = np.empty((2, 4096, D), dtype=np.float32)
    for c in range(8):
        b, q = c // 4, c % 4
        out[b, q * 1024:(q + 1) * 1024, :] = np.asarray(res.results[c]["out"]).T
    return out
